# version 01 (final)
# baseline (speedup 1.0000x reference)
import contextlib
import numpy as np
import ml_dtypes
import concourse.bass as bass
import concourse.mybir as mybir
from concourse.bass_utils import run_bass_kernel_spmd

F32 = mybir.dt.float32
BF16 = mybir.dt.bfloat16
I32 = mybir.dt.int32
U8 = mybir.dt.uint8
AF = mybir.ActivationFunctionType
ALU = mybir.AluOpType
AXX = mybir.AxisListType.X

D = 4096
T = 4096
BATCH = 4
NT = BATCH * T
HD = 128
NH = 32
INTER = 11008
NKC = INTER // 128
GW = 512
SCALE = 1.0 / (HD ** 0.5)
EXP_BIAS = -4.0
EPS = 1e-5
NV = [8, 16, 24, 32]
PHASE_LIMIT = 99
EXP_BARRIER = False
EXP_NO_WO = False
DBG = []


class _StopBuild(Exception):
    pass


def _sigma():
    return np.arange(INTER).reshape(43, 128, 2).transpose(0, 2, 1).reshape(-1)


class _MockInst:
    def then_inc(self, sem, n):
        return self

    def annotate(self, a):
        return self


class _MockEngine:
    def __getattr__(self, name):
        return lambda *a, **k: _MockInst()


_MOCK = _MockEngine()


class Ctr:
    def __init__(self, sem):
        self.sem = sem
        self.v = 0

    def inc(self, inst, n=1):
        if isinstance(inst, _MockInst):
            self.v += n
        else:
            inst.then_inc(self.sem, n)
        return inst


def build():
    nc = bass.Bass("TRN2", target_bir_lowering=False, debug=False)

    din = {}

    def inp(name, shape, dt):
        din[name] = nc.dram_tensor(name, shape, dt, kind="ExternalInput").ap()
        return din[name]

    xall = inp("xall", [NT, D], F32)
    xown = inp("xown", [2048, D], F32)
    wq = inp("wq", [D, D], F32)
    wkv_k = inp("wkv_k", [D, 256], F32)
    wkv_v = inp("wkv_v", [D, 256], F32)
    wo = inp("wo", [D, D], F32)
    bo = inp("bo", [1, D], F32)
    g_wqT = inp("g_wqT", [2048, INTER], I32)
    u_wqT = inp("u_wqT", [2048, INTER], I32)
    d_wqT = inp("d_wqT", [5504, D], I32)
    g_sT4 = inp("g_sT4", [4, 16 * INTER], F32)
    u_sT4 = inp("u_sT4", [4, 16 * INTER], F32)
    d_sT4e = inp("d_sT4e", [128, 43 * D], F32)
    g_sTf = inp("g_sTf", [64, INTER], F32)
    g_zTf = inp("g_zTf", [64, INTER], F32)
    u_sTf = inp("u_sTf", [64, INTER], F32)
    u_zTf = inp("u_zTf", [64, INTER], F32)
    d_sTa = inp("d_sTa", [128, D], F32)
    d_sTb = inp("d_sTb", [44, D], F32)
    d_zTa = inp("d_zTa", [128, D], F32)
    d_zTb = inp("d_zTb", [44, D], F32)
    identd = inp("identd", [128, 128], BF16)
    bigmask = inp("bigmask", [4 * 128, 32 * 512], BF16)
    ind4d = inp("ind4d", [4, 128], F32)
    f132d = inp("f132d", [128, 256], BF16)
    g66d = inp("g66d", [128, 130], BF16)
    onesd = inp("onesd", [128, 128], BF16)
    ones1f = inp("ones1f", [1, 128], F32)

    y = nc.dram_tensor("y", [2048, D], F32, kind="ExternalOutput").ap()

    s_xnT = nc.dram_tensor("s_xnT", [D, 2048], BF16, kind="Internal").ap()
    s_xnTe = nc.dram_tensor("s_xnTe", [2048, 2048], BF16, kind="Internal").ap()
    s_xnTo = nc.dram_tensor("s_xnTo", [2048, 2048], BF16, kind="Internal").ap()
    s_qT = nc.dram_tensor("s_qT", [D, 2048], BF16, kind="Internal").ap()
    s_k = nc.dram_tensor("s_k", [2 * BATCH * 128, T], BF16, kind="Internal").ap()
    s_v = nc.dram_tensor("s_v", [2 * BATCH * 128, T], BF16, kind="Internal").ap()
    s_ctxT = nc.dram_tensor("s_ctxT", [D, 2048], BF16, kind="Internal").ap()
    s_acc = nc.dram_tensor("s_acc", [2048, D], F32, kind="Internal").ap()
    s_hT = nc.dram_tensor("s_hT", [INTER, 2048], BF16, kind="Internal").ap()
    s_wob = nc.dram_tensor("s_wob", [D, D], BF16, kind="Internal").ap()
    s_wdT = nc.dram_tensor("s_wdT", [INTER, D], BF16, kind="Internal").ap()
    s_zs = nc.dram_tensor("s_zs", [64, 2 * INTER], BF16, kind="Internal").ap()
    s_zsd = nc.dram_tensor("s_zsd", [172, D], BF16, kind="Internal").ap()

    prog = {e: [] for e in ("tensor", "vector", "scalar", "gpsimd", "sync")}

    def E(eng, fn):
        prog[eng].append(fn)
        fn(_MOCK)

    with contextlib.ExitStack() as top:
        sems = {}
        for name in ("dS", "dG", "tD", "vD", "sD", "gD", "bar"):
            sems[name] = Ctr(top.enter_context(nc.semaphore(name)))
        dS, dG, tD, vD, sD, gD, bar = (sems[n] for n in
                                       ("dS", "dG", "tD", "vD", "sD", "gD",
                                        "bar"))

        ident = top.enter_context(nc.sbuf_tensor("c_ident", [128, 128], BF16))
        ind4 = top.enter_context(nc.sbuf_tensor("c_ind4", [4, 128], F32))
        f132 = top.enter_context(nc.sbuf_tensor("c_f132", [128, 256], BF16))
        g66 = top.enter_context(nc.sbuf_tensor("c_g66", [128, 130], BF16))
        ones = top.enter_context(nc.sbuf_tensor("c_ones", [128, 128], BF16))
        ones1 = top.enter_context(nc.sbuf_tensor("c_ones1", [1, 128], F32))
        sxn = top.enter_context(nc.sbuf_tensor("c_sxn", [64, 2048], BF16))

        dmasems = {}

        def dma(eng, name, out, in_):
            if name not in dmasems:
                dmasems[name] = Ctr(
                    top.enter_context(nc.semaphore(f"q{len(dmasems)}")))
            ctr = dmasems[name]

            def fn(e, out=out, in_=in_, ctr=ctr):
                ctr.inc(e.dma_start(out=out, in_=in_), 16)
            E(eng, fn)
            return ctr

        def drain_dma(eng):
            for ctr in dmasems.values():
                W(eng, ctr, ctr.v)

        def W(eng, ctr, val):
            if val > 0:
                E(eng, lambda e, s=ctr.sem, v=val: e.wait_ge(s, v))

        def barrier():
            for eng in prog:
                E(eng, lambda e: bar.inc(e.nop()))
            tgt = bar.v
            for eng in prog:
                W(eng, bar, tgt)

        dma("sync", "consts", ident[:], identd)
        dma("sync", "consts", ind4[:], ind4d)
        dma("sync", "consts", f132[:], f132d)
        dma("sync", "consts", g66[:], g66d)
        dma("sync", "consts", ones[:], onesd)
        dma("sync", "consts", ones1[:], ones1f)
        _cc = dmasems["consts"]
        for _eng in ("tensor", "vector", "scalar", "gpsimd"):
            W(_eng, _cc, _cc.v)

        dbg_tensors = {
            "s_xnT": (s_xnT, [D, 2048], BF16),
            "s_xnTe": (s_xnTe, [2048, 2048], BF16),
            "s_xnTo": (s_xnTo, [2048, 2048], BF16),
            "s_qT": (s_qT, [D, 2048], BF16),
            "s_k": (s_k, [2 * BATCH * 128, T], BF16),
            "s_v": (s_v, [2 * BATCH * 128, T], BF16),
            "s_ctxT": (s_ctxT, [D, 2048], BF16),
            "s_acc": (s_acc, [2048, D], F32),
            "s_hT": (s_hT, [INTER, 2048], BF16),
            "s_wob": (s_wob, [D, D], BF16),
            "s_wdT": (s_wdT, [INTER, D], BF16),
            "s_zs": (s_zs, [64, 2 * INTER], BF16),
            "s_zsd": (s_zsd, [172, D], BF16),
        }

        def _run_phases():
            def _phase1():
                with contextlib.ExitStack() as sc:
                    xbuf = [sc.enter_context(nc.sbuf_tensor(f"a_x{i}", [128, D], F32))
                            for i in range(2)]
                    xnbf = [sc.enter_context(nc.sbuf_tensor(f"a_xn{i}", [128, D], BF16))
                            for i in range(2)]
                    lnsm = [sc.enter_context(nc.sbuf_tensor(f"a_ln{i}", [128, 64], F32))
                            for i in range(2)]
                    xnT = sc.enter_context(nc.sbuf_tensor("a_xnT", [128, 32 * 512], BF16))
                    wkk = sc.enter_context(nc.sbuf_tensor("a_wkk", [128, 32 * 256], BF16))
                    wvv = sc.enter_context(nc.sbuf_tensor("a_wvv", [128, 32 * 256], BF16))
                    kstg = sc.enter_context(nc.sbuf_tensor("a_kstg", [128, 1024], BF16))
                    vstg = sc.enter_context(nc.sbuf_tensor("a_vstg", [128, 2048], BF16))
                    wof = [sc.enter_context(nc.sbuf_tensor(f"a_wof{i}", [128, D], F32))
                           for i in range(2)]
                    wob = [sc.enter_context(nc.sbuf_tensor(f"a_wob{i}", [128, D], BF16))
                           for i in range(2)]
                    ps_tr = [sc.enter_context(nc.psum_tensor(f"a_ptr{i}", [128, 512], F32))
                             for i in range(2)]
                    ps_k = [sc.enter_context(nc.psum_tensor(f"a_pk{i}", [128, 512], F32))
                            for i in range(2)]
                    ps_v = [sc.enter_context(nc.psum_tensor(f"a_pv{i}", [128, 512], F32))
                            for i in range(2)]
                    sqj = sc.enter_context(nc.sbuf_tensor("a_sqj", [128, D], F32))

                    c0_ = dma("sync", "wofA0", wof[0][:].rearrange("p (a c) -> p a c", c=256)[:, 0:16, :],
                        wkv_k.rearrange("(a p) c -> p a c", p=128)[:, 0:16, :])
                    c1_ = dma("sync", "wofA1", wof[1][:].rearrange("p (a c) -> p a c", c=256)[:, 0:16, :],
                        wkv_k.rearrange("(a p) c -> p a c", p=128)[:, 16:32, :])
                    W("vector", c0_, c0_.v)
                    W("vector", c1_, c1_.v)
                    E("vector", lambda v: v.tensor_copy(wkk[:, 0:4096], wof[0][:, 0:4096]))
                    E("vector", lambda v: vD.inc(v.tensor_copy(wkk[:, 4096:8192],
                                                               wof[1][:, 0:4096])))
                    mk1 = vD.v
                    W("sync", vD, mk1)
                    c0_ = dma("sync", "wofA0", wof[0][:].rearrange("p (a c) -> p a c", c=256)[:, 0:16, :],
                        wkv_v.rearrange("(a p) c -> p a c", p=128)[:, 0:16, :])
                    c1_ = dma("sync", "wofA1", wof[1][:].rearrange("p (a c) -> p a c", c=256)[:, 0:16, :],
                        wkv_v.rearrange("(a p) c -> p a c", p=128)[:, 16:32, :])
                    W("vector", c0_, c0_.v)
                    W("vector", c1_, c1_.v)
                    E("vector", lambda v: v.tensor_copy(wvv[:, 0:4096], wof[0][:, 0:4096]))
                    E("vector", lambda v: vD.inc(v.tensor_copy(wvv[:, 4096:8192],
                                                               wof[1][:, 0:4096])))
                    mk2 = vD.v
                    W("sync", vD, mk2)

                    wo_tiles_emitted = 0
                    wo_conv_done = [mk1, mk2]
                    wo_spill = [None, None]

                    def emit_wo_tile():
                        nonlocal wo_tiles_emitted
                        i = wo_tiles_emitted
                        if i >= 32:
                            return
                        if EXP_NO_WO:
                            wo_tiles_emitted += 1
                            return
                        sl = i % 2
                        W("sync", vD, wo_conv_done[sl])
                        lc = dma("sync", f"wofA{sl}", wof[sl][:],
                                 wo[i * 128:(i + 1) * 128, :])
                        W("vector", lc, lc.v)
                        if wo_spill[sl] is not None:
                            W("vector", *wo_spill[sl])
                        E("vector", lambda v, sl=sl: vD.inc(
                            v.tensor_copy(wob[sl][:], wof[sl][:])))
                        wo_conv_done[sl] = vD.v
                        W("gpsimd", vD, wo_conv_done[sl])
                        pc = dma("gpsimd", f"wobA{sl}",
                                 s_wob[i * 128:(i + 1) * 128, :], wob[sl][:])
                        wo_spill[sl] = (pc, pc.v)
                        wo_tiles_emitted += 1

                    xfree = [None, None]
                    xn_used = [0, 0]
                    tr_ring = 0
                    tr_evac = [0, 0]
                    k_evac = [0, 0]
                    v_evac = [0, 0]
                    kstg_spilled = None
                    vstg_spilled = None
                    xnT_consumed_t = 0
                    sqj_war = 0

                    for ti in range(NT // 128):
                        s = ti % 2
                        bb = ti // 32
                        bw = (ti % 32) // 4
                        iw = ti % 4

                        if xfree[s] is not None:
                            g_, v_, q_ = xfree[s]
                            W("sync", gD, g_)
                            W("sync", vD, v_)
                            W("sync", sD, q_)
                        xc = dma("sync", f"xA{s}", xbuf[s][:],
                                 xall[ti * 128:(ti + 1) * 128, :])
                        xl = xc.v

                        W("vector", xc, xl)
                        E("vector", lambda v, s=s: v.reduce_sum(
                            lnsm[s][:, 0:1], xbuf[s][:], axis=AXX))
                        E("vector", lambda v: v.drain())
                        E("vector", lambda v, s=s: v.tensor_scalar(
                            lnsm[s][:, 4:5], lnsm[s][:, 0:1], 1.0 / D, None,
                            op0=ALU.mult))
                        E("vector", lambda v: v.drain())
                        E("vector", lambda v, s=s: vD.inc(v.tensor_scalar(
                            xbuf[s][:], xbuf[s][:], lnsm[s][:, 4:5], None,
                            op0=ALU.subtract)))
                        v_ctr = vD.v
                        W("scalar", vD, v_ctr)
                        E("scalar", lambda sc_, s=s: sD.inc(sc_.activation(
                            sqj[:], xbuf[s][:], AF.Square)))
                        s_sq = sD.v
                        W("vector", sD, s_sq)
                        E("vector", lambda v, s=s: v.reduce_sum(
                            lnsm[s][:, 1:2], sqj[:], axis=AXX))
                        E("vector", lambda v: v.drain())
                        E("vector", lambda v, s=s: v.tensor_scalar(
                            lnsm[s][:, 5:6], lnsm[s][:, 1:2], 1.0 / D, EPS,
                            op0=ALU.mult, op1=ALU.add))
                        E("vector", lambda v: vD.inc(v.drain()))
                        v_var = vD.v
                        W("scalar", vD, v_var)
                        E("scalar", lambda sc_, s=s: sc_.activation(
                            lnsm[s][:, 8:9], lnsm[s][:, 5:6], AF.Sqrt))
                        E("scalar", lambda sc_: sD.inc(sc_.drain()))
                        s_sd = sD.v
                        W("vector", sD, s_sd)
                        E("vector", lambda v, s=s: v.reciprocal(
                            lnsm[s][:, 12:13], lnsm[s][:, 8:9]))
                        E("vector", lambda v: vD.inc(v.drain()))
                        v_r = vD.v
                        W("gpsimd", vD, v_r)
                        W("gpsimd", tD, xn_used[s])
                        E("gpsimd", lambda g, s=s: gD.inc(g.tensor_scalar(
                            xnbf[s][:], xbuf[s][:], lnsm[s][:, 12:13], None,
                            op0=ALU.mult)))
                        g_ln = gD.v
                        xfree[s] = (g_ln, v_var, s_sd)

                        W("tensor", gD, g_ln)
                        for grp in range(8):
                            slot = tr_ring % 2
                            tr_ring += 1
                            W("tensor", sD, tr_evac[slot])
                            for j4 in range(4):
                                j = grp * 4 + j4
                                def fmm(t, s=s, j=j, j4=j4, slot=slot):
                                    inst = t.matmul(
                                        ps_tr[slot][:, j4 * 128:(j4 + 1) * 128],
                                        xnbf[s][:, j * 128:(j + 1) * 128], ident[:],
                                        start=True, stop=True)
                                    if j4 == 3:
                                        tD.inc(inst)
                                E("tensor", fmm)
                            t_tr = tD.v
                            W("scalar", tD, t_tr)
                            if iw == 0 and grp == 0:
                                W("scalar", tD, xnT_consumed_t)

                            def fev(sc_, slot=slot, grp=grp, iw=iw):
                                dst = xnT[:].rearrange("p (j c) -> p j c", c=512)[
                                    :, grp * 4:(grp + 1) * 4, iw * 128:(iw + 1) * 128]
                                src = ps_tr[slot][:].rearrange("p (j c) -> p j c", c=128)
                                sD.inc(sc_.copy(dst, src))
                            E("scalar", fev)
                            tr_evac[slot] = sD.v

                        xn_used[s] = tD.v
                        emit_wo_tile()
                        if EXP_BARRIER:
                            barrier()

                        if iw == 3:
                            if ti == 3:
                                W("tensor", vD, mk2)
                            W("tensor", sD, max(tr_evac))
                            for gi in range(2):
                                slot = gi
                                W("tensor", vD, k_evac[slot])
                                for j in range(32):
                                    def fmm(t, gi=gi, j=j, slot=slot):
                                        inst = t.matmul(
                                            ps_k[slot][:],
                                            wkk[:, j * 256 + gi * 128:
                                                j * 256 + gi * 128 + 128],
                                            xnT[:, j * 512:(j + 1) * 512],
                                            start=(j == 0), stop=(j == 31))
                                        if j == 31:
                                            tD.inc(inst)
                                    E("tensor", fmm)
                                t_k = tD.v
                                W("vector", tD, t_k)
                                if kstg_spilled is not None:
                                    W("vector", *kstg_spilled)
                                E("vector", lambda v, slot=slot, gi=gi: vD.inc(
                                    v.tensor_copy(kstg[:, gi * 512:(gi + 1) * 512],
                                                  ps_k[slot][:])))
                                k_evac[slot] = vD.v
                            for i4 in range(4):
                                slot = i4 % 2
                                W("tensor", vD, v_evac[slot])
                                for j in range(32):
                                    def fmm(t, i4=i4, j=j, slot=slot):
                                        inst = t.matmul(
                                            ps_v[slot][:, 0:256],
                                            xnT[:, j * 512 + i4 * 128:
                                                j * 512 + i4 * 128 + 128],
                                            wvv[:, j * 256:(j + 1) * 256],
                                            start=(j == 0), stop=(j == 31))
                                        if j == 31:
                                            tD.inc(inst)
                                    E("tensor", fmm)
                                t_v = tD.v
                                W("vector", tD, t_v)
                                if vstg_spilled is not None:
                                    W("vector", *vstg_spilled)
                                E("vector", lambda v, slot=slot, i4=i4: vD.inc(
                                    v.tensor_copy(vstg[:, i4 * 512:i4 * 512 + 256],
                                                  ps_v[slot][:, 0:256])))
                                v_evac[slot] = vD.v
                            xnT_consumed_t = tD.v

                            W("gpsimd", vD, max(k_evac))
                            for gi in range(2):
                                kc = dma("gpsimd", "kspillA",
                                    s_k[(gi * 4 + bb) * 128:(gi * 4 + bb + 1) * 128,
                                        bw * 512:(bw + 1) * 512],
                                    kstg[:, gi * 512:(gi + 1) * 512])
                            kstg_spilled = (kc, kc.v)
                            W("gpsimd", vD, max(v_evac))
                            for gi in range(2):
                                src = vstg[:].rearrange("p (i c) -> p i c", c=512)[
                                    :, :, gi * 128:(gi + 1) * 128]
                                dst = s_v[(gi * 4 + bb) * 128:(gi * 4 + bb + 1) * 128, :]\
                                    .rearrange("p (i c) -> p i c", c=128)[
                                    :, 4 * bw:4 * bw + 4, :]
                                vc = dma("gpsimd", "vspillA", dst, src)
                            vstg_spilled = (vc, vc.v)

                    while wo_tiles_emitted < 32:
                        emit_wo_tile()
                    drain_dma("gpsimd")
                    drain_dma("sync")
                    barrier()
            _phase1()

            if PHASE_LIMIT <= 1:
                raise _StopBuild()

            def _phase2():
                with contextlib.ExitStack() as sc:
                    xbuf = [sc.enter_context(nc.sbuf_tensor(f"b_x{i}", [128, D], F32))
                            for i in range(2)]
                    xnbf = [sc.enter_context(nc.sbuf_tensor(f"b_xn{i}", [128, D], BF16))
                            for i in range(2)]
                    lnsm = [sc.enter_context(nc.sbuf_tensor(f"b_ln{i}", [128, 64], F32))
                            for i in range(2)]
                    xnT = sc.enter_context(nc.sbuf_tensor("b_xnT", [128, 32 * 512], BF16))
                    xnTe = sc.enter_context(nc.sbuf_tensor("b_xnTe", [128, 16 * 512], BF16))
                    xnTo = sc.enter_context(nc.sbuf_tensor("b_xnTo", [128, 16 * 512], BF16))
                    ps_tr = [sc.enter_context(nc.psum_tensor(f"b_ptr{i}", [128, 512], F32))
                             for i in range(2)]
                    ps_sx = sc.enter_context(nc.psum_tensor("b_psx", [64, 512], F32))
                    sqj = sc.enter_context(nc.sbuf_tensor("b_sqj", [128, D], F32))

                    xfree = [None, None]
                    xn_used = [0, 0]
                    tr_ring = 0
                    tr_evac = [0, 0]
                    win_spills = []
                    xnT_cons_t = 0

                    for ti in range(16):
                        s = ti % 2
                        ow = ti // 4
                        iw = ti % 4
                        if xfree[s] is not None:
                            g_, v_, q_ = xfree[s]
                            W("sync", gD, g_)
                            W("sync", vD, v_)
                            W("sync", sD, q_)
                        xc = dma("sync", f"xB{s}", xbuf[s][:],
                                 xown[ti * 128:(ti + 1) * 128, :])
                        xl = xc.v
                        W("vector", xc, xl)
                        E("vector", lambda v, s=s: v.reduce_sum(
                            lnsm[s][:, 0:1], xbuf[s][:], axis=AXX))
                        E("vector", lambda v: v.drain())
                        E("vector", lambda v, s=s: v.tensor_scalar(
                            lnsm[s][:, 4:5], lnsm[s][:, 0:1], 1.0 / D, None,
                            op0=ALU.mult))
                        E("vector", lambda v: v.drain())
                        E("vector", lambda v, s=s: vD.inc(v.tensor_scalar(
                            xbuf[s][:], xbuf[s][:], lnsm[s][:, 4:5], None,
                            op0=ALU.subtract)))
                        v_ctr = vD.v
                        W("scalar", vD, v_ctr)
                        E("scalar", lambda sc_, s=s: sD.inc(sc_.activation(
                            sqj[:], xbuf[s][:], AF.Square)))
                        s_sq = sD.v
                        W("vector", sD, s_sq)
                        E("vector", lambda v, s=s: v.reduce_sum(
                            lnsm[s][:, 1:2], sqj[:], axis=AXX))
                        E("vector", lambda v: v.drain())
                        E("vector", lambda v, s=s: v.tensor_scalar(
                            lnsm[s][:, 5:6], lnsm[s][:, 1:2], 1.0 / D, EPS,
                            op0=ALU.mult, op1=ALU.add))
                        E("vector", lambda v: vD.inc(v.drain()))
                        v_var = vD.v
                        W("scalar", vD, v_var)
                        E("scalar", lambda sc_, s=s: sc_.activation(
                            lnsm[s][:, 8:9], lnsm[s][:, 5:6], AF.Sqrt))
                        E("scalar", lambda sc_: sD.inc(sc_.drain()))
                        s_sd = sD.v
                        W("vector", sD, s_sd)
                        E("vector", lambda v, s=s: v.reciprocal(
                            lnsm[s][:, 12:13], lnsm[s][:, 8:9]))
                        E("vector", lambda v: vD.inc(v.drain()))
                        v_r = vD.v
                        W("gpsimd", vD, v_r)
                        W("gpsimd", tD, xn_used[s])
                        E("gpsimd", lambda g, s=s: gD.inc(g.tensor_scalar(
                            xnbf[s][:], xbuf[s][:], lnsm[s][:, 12:13], None,
                            op0=ALU.mult)))
                        g_ln = gD.v
                        xfree[s] = (g_ln, v_var, s_sd)

                        W("tensor", gD, g_ln)
                        for kind, buf, njt in (("nat", xnT, 8), ("e", xnTe, 4),
                                               ("o", xnTo, 4)):
                            par = 0 if kind != "o" else 1
                            for grp in range(njt):
                                slot = tr_ring % 2
                                tr_ring += 1
                                W("tensor", sD, tr_evac[slot])
                                for j4 in range(4):
                                    j = grp * 4 + j4
                                    def fmm(t, s=s, j=j, j4=j4, slot=slot, kind=kind,
                                            par=par):
                                        if kind == "nat":
                                            lhs = xnbf[s][:, j * 128:(j + 1) * 128]
                                        else:
                                            lhs = xnbf[s][:, 256 * j + par:
                                                          256 * (j + 1) + par - 1:2]
                                        inst = t.matmul(
                                            ps_tr[slot][:, j4 * 128:(j4 + 1) * 128],
                                            lhs, ident[:], start=True, stop=True)
                                        if j4 == 3:
                                            tD.inc(inst)
                                    E("tensor", fmm)
                                t_tr = tD.v
                                W("scalar", tD, t_tr)
                                if iw == 0 and grp == 0 and kind == "nat":
                                    W("scalar", tD, xnT_cons_t)
                                    for _ws in win_spills:
                                        W("scalar", *_ws)

                                def fev(sc_, slot=slot, grp=grp, iw=iw, buf=buf):
                                    dst = buf[:].rearrange("p (j c) -> p j c", c=512)[
                                        :, grp * 4:(grp + 1) * 4,
                                        iw * 128:(iw + 1) * 128]
                                    src = ps_tr[slot][:].rearrange(
                                        "p (j c) -> p j c", c=128)
                                    sD.inc(sc_.copy(dst, src))
                                E("scalar", fev)
                                tr_evac[slot] = sD.v

                        xn_used[s] = tD.v
                        if iw == 3:
                            W("tensor", sD, max(tr_evac))
                            for j in range(32):
                                def fmm(t, j=j):
                                    inst = t.matmul(
                                        ps_sx[:, :],
                                        g66[:, 64 - 2 * j:128 - 2 * j],
                                        xnT[:, j * 512:(j + 1) * 512],
                                        start=(j == 0), stop=(j == 31))
                                    if j == 31:
                                        tD.inc(inst)
                                E("tensor", fmm)
                            t_sx = tD.v
                            xnT_cons_t = tD.v
                            W("vector", tD, t_sx)
                            E("vector", lambda v, ow=ow: vD.inc(v.tensor_copy(
                                sxn[:, ow * 512:(ow + 1) * 512], ps_sx[:])))
                            v_sx = vD.v
                            W("gpsimd", sD, max(tr_evac))
                            sp1 = dma("gpsimd", "xnTsp",
                                s_xnT.rearrange("(j p) t -> p j t", p=128)[
                                    :, :, ow * 512:(ow + 1) * 512],
                                xnT[:].rearrange("p (j t) -> p j t", t=512))
                            sp2 = dma("gpsimd", "xnTesp",
                                s_xnTe.rearrange("(j p) t -> p j t", p=128)[
                                    :, :, ow * 512:(ow + 1) * 512],
                                xnTe[:].rearrange("p (j t) -> p j t", t=512))
                            sp3 = dma("gpsimd", "xnTosp",
                                s_xnTo.rearrange("(j p) t -> p j t", p=128)[
                                    :, :, ow * 512:(ow + 1) * 512],
                                xnTo[:].rearrange("p (j t) -> p j t", t=512))
                            win_spills = [(sp1, sp1.v), (sp2, sp2.v), (sp3, sp3.v)]

                    drain_dma("gpsimd")
                    drain_dma("sync")
                    barrier()
            _phase2()

            if PHASE_LIMIT <= 2:
                raise _StopBuild()

            def _phase3():
                with contextlib.ExitStack() as sc:
                    xnTh = sc.enter_context(nc.sbuf_tensor("b_xnTh", [128, 32 * 1024], BF16))
                    wqf = [sc.enter_context(nc.sbuf_tensor(f"b_wqf{i}", [128, 32 * 128], F32))
                           for i in range(2)]
                    wqb = [sc.enter_context(nc.sbuf_tensor(f"b_wqb{i}", [128, 32 * 128], BF16))
                           for i in range(2)]
                    qstg = [sc.enter_context(nc.sbuf_tensor(f"b_qs{i}", [128, 1024], BF16))
                            for i in range(2)]
                    ps_q = [sc.enter_context(nc.psum_tensor(f"b_pq{i}", [128, 1024], F32))
                            for i in range(2)]
                    q_conv = [0, 0]
                    q_evac = [0, 0]
                    q_spill = [None, None]
                    q_mm = [0, 0]
                    for th in range(2):
                        W("sync", tD, tD.v)
                        xhc = dma("sync", "xnTh",
                            xnTh[:].rearrange("p (j c) -> p j c", c=1024),
                            s_xnT.rearrange("(j p) t -> p j t", p=128)[
                                :, :, th * 1024:(th + 1) * 1024])
                        xh = xhc.v
                        for head in range(NH):
                            hs = head % 2
                            W("sync", vD, q_conv[hs])
                            wqc = dma("sync", f"wqf{hs}",
                                wqf[hs][:].rearrange("p (a c) -> p a c", c=128),
                                wq.rearrange("(a p) c -> p a c", p=128)[
                                    :, :, head * 128:(head + 1) * 128])
                            wql = wqc.v
                            W("vector", wqc, wql)
                            W("vector", tD, q_mm[hs])
                            E("vector", lambda v, hs=hs: vD.inc(
                                v.tensor_scalar(wqb[hs][:], wqf[hs][:], SCALE, None,
                                                op0=ALU.mult)))
                            q_conv[hs] = vD.v
                            W("tensor", vD, q_conv[hs])
                            W("tensor", xhc, xh)
                            W("tensor", sD, q_evac[hs])
                            for j in range(32):
                                for nw in range(2):
                                    def fmm(t, hs=hs, j=j, nw=nw):
                                        inst = t.matmul(
                                            ps_q[hs][:, nw * 512:(nw + 1) * 512],
                                            wqb[hs][:, j * 128:(j + 1) * 128],
                                            xnTh[:, j * 1024 + nw * 512:
                                                 j * 1024 + (nw + 1) * 512],
                                            start=(j == 0), stop=(j == 31))
                                        if j == 31 and nw == 1:
                                            tD.inc(inst)
                                    E("tensor", fmm)
                            t_q = tD.v
                            q_mm[hs] = tD.v
                            W("scalar", tD, t_q)
                            if q_spill[hs] is not None:
                                W("scalar", *q_spill[hs])
                            E("scalar", lambda sc_, hs=hs: sD.inc(
                                sc_.copy(qstg[hs][:], ps_q[hs][:])))
                            q_evac[hs] = sD.v
                            W("gpsimd", sD, q_evac[hs])
                            qc = dma("gpsimd", f"qsp{hs}",
                                s_qT[head * 128:(head + 1) * 128,
                                     th * 1024:(th + 1) * 1024], qstg[hs][:])
                            q_spill[hs] = (qc, qc.v)
                    drain_dma("gpsimd")
                    drain_dma("sync")
                    barrier()
            _phase3()

            if PHASE_LIMIT <= 3:
                raise _StopBuild()

            def _phase4():
                with contextlib.ExitStack() as sc:
                    k_sb = sc.enter_context(nc.sbuf_tensor("c_k", [128, 4 * T], BF16))
                    v_sb = sc.enter_context(nc.sbuf_tensor("c_v", [128, 4 * T], BF16))
                    qwin = sc.enter_context(nc.sbuf_tensor("c_q", [128, 16 * 512], BF16))
                    maskw = sc.enter_context(nc.sbuf_tensor("c_m", [128, 32 * 512], BF16))
                    expS = [sc.enter_context(nc.sbuf_tensor(f"c_e{i}", [128, 512], BF16))
                            for i in range(4)]
                    dens = sc.enter_context(nc.sbuf_tensor("c_den", [1, 512], F32))
                    rden = sc.enter_context(nc.sbuf_tensor("c_rd", [128, 512], F32))
                    ctxb = [sc.enter_context(
                        nc.sbuf_tensor(f"c_cb{i}", [128, 16 * 512], BF16))
                        for i in range(2)]
                    ps_s = [sc.enter_context(nc.psum_tensor(f"c_ps{i}", [128, 512], F32))
                            for i in range(2)]
                    ps_c = [sc.enter_context(nc.psum_tensor(f"c_pc{i}", [128, 512], F32))
                            for i in range(2)]
                    ps_d = sc.enter_context(nc.psum_tensor("c_pd", [128, 512], F32))
                    ps_r = sc.enter_context(nc.psum_tensor("c_pr", [128, 512], F32))

                    exp_done = [0, 0]
                    exp_done_eng = [None, None]
                    exp_slot_free = [0, 0, 0, 0]
                    ctx_done = [0, 0]
                    den_free = 0
                    rden_free = 0
                    ctx_spill = [None, None]

                    for gh in range(2):
                        W("sync", tD, tD.v)
                        kbc = dma("sync", "ksb", k_sb[:].rearrange("p (b t) -> p b t", t=T),
                            s_k.rearrange("(b p) t -> p b t", p=128)[:, gh * 4:gh * 4 + 4, :])
                        vbc = dma("sync", "vsb", v_sb[:].rearrange("p (b t) -> p b t", t=T),
                            s_v.rearrange("(b p) t -> p b t", p=128)[:, gh * 4:gh * 4 + 4, :])
                        for w in range(4):
                            nv = NV[w]
                            W("sync", tD, tD.v)
                            qwc = dma("sync", "qwin",
                                qwin[:].rearrange("p (h c) -> p h c", c=512),
                                s_qT.rearrange("(h p) t -> p h t", p=128)[
                                    :, gh * 16:(gh + 1) * 16, w * 512:(w + 1) * 512])
                            mwc = dma("sync", "maskw", maskw[:],
                                bigmask[w * 128:(w + 1) * 128, :])
                            W("tensor", kbc, kbc.v)
                            W("tensor", vbc, vbc.v)
                            W("tensor", qwc, qwc.v)
                            W("vector", mwc, mwc.v)
                            for hi in range(16):
                                h = gh * 16 + hi
                                bbh = (h // 4) % 4
                                cs = hi % 2
                                W("tensor", vD, ctx_done[cs])
                                for kk in range(nv):
                                    ss = kk % 2
                                    es = kk % 4
                                    if exp_done[ss]:
                                        W("tensor", exp_done_eng[ss], exp_done[ss])
                                    def fs(t, bbh=bbh, kk=kk, ss=ss, hi=hi):
                                        tD.inc(t.matmul(
                                            ps_s[ss][:],
                                            k_sb[:, bbh * T + kk * 128:
                                                 bbh * T + (kk + 1) * 128],
                                            qwin[:, hi * 512:(hi + 1) * 512],
                                            start=True, stop=True))
                                    E("tensor", fs)
                                    t_s = tD.v
                                    W("scalar", tD, t_s)
                                    W("scalar", tD, exp_slot_free[es])
                                    E("scalar", lambda sc_, ss=ss, es=es: sD.inc(
                                        sc_.activation(expS[es][:], ps_s[ss][:],
                                                       AF.Exp)))
                                    s_e = sD.v
                                    W("vector", sD, s_e)
                                    E("vector", lambda v, es=es, kk=kk: vD.inc(
                                        v.tensor_tensor(expS[es][:], expS[es][:],
                                                        maskw[:, kk * 512:(kk + 1) * 512],
                                                        op=ALU.mult)))
                                    exp_done[ss] = vD.v
                                    exp_done_eng[ss] = vD
                                    W("tensor", vD, exp_done[ss])
                                    if kk == 0:
                                        W("tensor", vD, den_free)
                                    def fd(t, kk=kk, es=es, nv=nv):
                                        t.matmul(ps_d[0:1, :], ones[:, 0:1], expS[es][:],
                                                 start=(kk == 0), stop=(kk == nv - 1))
                                    E("tensor", fd)
                                    def fc(t, kk=kk, es=es, nv=nv, cs=cs, bbh=bbh):
                                        tD.inc(t.matmul(
                                            ps_c[cs][:],
                                            v_sb[:, bbh * T + kk * 128:
                                                 bbh * T + (kk + 1) * 128],
                                            expS[es][:],
                                            start=(kk == 0), stop=(kk == nv - 1)))
                                    E("tensor", fc)
                                    exp_slot_free[es] = tD.v
                                t_ctx = tD.v
                                W("vector", tD, t_ctx)
                                E("vector", lambda v: vD.inc(
                                    v.reciprocal(dens[:], ps_d[0:1, :])))
                                den_free = vD.v
                                W("tensor", vD, den_free)
                                if rden_free:
                                    W("tensor", vD, rden_free)
                                def fr(t):
                                    tD.inc(t.matmul(ps_r[:], ones1[:], dens[:],
                                                    start=True, stop=True))
                                E("tensor", fr)
                                t_r = tD.v
                                W("vector", tD, t_r)
                                E("vector", lambda v: v.tensor_copy(rden[:], ps_r[:]))
                                if ctx_spill[w % 2] is not None:
                                    W("vector", *ctx_spill[w % 2])
                                E("vector", lambda v, cs=cs, hi=hi, w=w: vD.inc(
                                    v.tensor_tensor(ctxb[w % 2][:, hi * 512:(hi + 1) * 512],
                                                    ps_c[cs][:], rden[:], op=ALU.mult)))
                                rden_free = vD.v
                                ctx_done[cs] = vD.v
                            W("gpsimd", vD, max(ctx_done))
                            cc = dma("gpsimd", f"ctxsp{w % 2}",
                                s_ctxT.rearrange("(h p) t -> p h t", p=128)[
                                    :, gh * 16:(gh + 1) * 16, w * 512:(w + 1) * 512],
                                ctxb[w % 2][:].rearrange("p (h c) -> p h c", c=512))
                            ctx_spill[w % 2] = (cc, cc.v)
                    drain_dma("gpsimd")
                    drain_dma("sync")
                    barrier()
            _phase4()

            if PHASE_LIMIT <= 4:
                raise _StopBuild()

            def _phase5():
                with contextlib.ExitStack() as sc:
                    ctxw = sc.enter_context(nc.sbuf_tensor("d_ctx", [128, 32 * 512], BF16))
                    wobt = [sc.enter_context(
                        nc.sbuf_tensor(f"d_wob{i}", [128, 32 * 128], BF16))
                        for i in range(2)]
                    attnw = sc.enter_context(nc.sbuf_tensor("d_at", [128, 32 * 512], BF16))
                    xw = [sc.enter_context(nc.sbuf_tensor(f"d_x{i}", [128, D], F32))
                          for i in range(2)]
                    accst = [sc.enter_context(nc.sbuf_tensor(f"d_ac{i}", [128, D], F32))
                             for i in range(2)]
                    prepf = sc.enter_context(nc.sbuf_tensor("d_pp", [128, D], F32))
                    zstmp = sc.enter_context(nc.sbuf_tensor("d_zst", [128, D], BF16))
                    ps_o = [sc.enter_context(nc.psum_tensor(f"d_po{i}", [128, 512], F32))
                            for i in range(2)]
                    ps_t = [sc.enter_context(nc.psum_tensor(f"d_pt{i}", [128, 512], F32))
                            for i in range(2)]

                    zspill = None
                    for wi, (sTf_, zTf_) in enumerate(((g_sTf, g_zTf), (u_sTf, u_zTf))):
                        for c0 in range(0, INTER, 2048):
                            cw = min(2048, INTER - c0)
                            W("sync", vD, vD.v)
                            p1 = dma("sync", "prepS", prepf[0:64, 0:cw], sTf_[:, c0:c0 + cw])
                            p2 = dma("sync", "prepZ", prepf[0:64, 2048:2048 + cw], zTf_[:, c0:c0 + cw])
                            W("vector", p1, p1.v)
                            W("vector", p2, p2.v)
                            if zspill is not None:
                                W("vector", *zspill)
                            E("vector", lambda v, cw=cw: vD.inc(
                                v.scalar_tensor_tensor(
                                    zstmp[0:64, 0:cw],
                                    prepf[0:64, 0:cw], -1.0, prepf[0:64, 2048:2048 + cw],
                                    op0=ALU.mult, op1=ALU.mult)))
                            W("gpsimd", vD, vD.v)
                            zc = dma("gpsimd", "zssp",
                                s_zs[:, wi * INTER + c0:wi * INTER + c0 + cw],
                                zstmp[0:64, 0:cw])
                            zspill = (zc, zc.v)
                    for part, sT_, zT_, r0, rn in (("A", d_sTa, d_zTa, 0, 128),
                                                   ("B", d_sTb, d_zTb, 128, 44)):
                        for c0 in (0, 2048):
                            W("sync", vD, vD.v)
                            p1 = dma("sync", "prepS", prepf[0:rn, 0:2048], sT_[:, c0:c0 + 2048])
                            p2 = dma("sync", "prepZ", prepf[0:rn, 2048:4096], zT_[:, c0:c0 + 2048])
                            W("vector", p1, p1.v)
                            W("vector", p2, p2.v)
                            if zspill is not None:
                                W("vector", *zspill)
                            E("vector", lambda v, rn=rn: vD.inc(
                                v.scalar_tensor_tensor(
                                    zstmp[0:rn, 0:2048], prepf[0:rn, 0:2048], -1.0,
                                    prepf[0:rn, 2048:4096], op0=ALU.mult, op1=ALU.mult)))
                            W("gpsimd", vD, vD.v)
                            zc = dma("gpsimd", "zssp", s_zsd[r0:r0 + rn, c0:c0 + 2048],
                                zstmp[0:rn, 0:2048])
                            zspill = (zc, zc.v)

                    o_evac = [0, 0]
                    t_free = [0, 0]
                    wob_used = [0, 0]
                    x_used = [0, 0]
                    acc_spill = [None, None]
                    ctx_used_t = 0

                    for w in range(4):
                        W("sync", tD, ctx_used_t)
                        clc = dma("sync", "ctxw",
                            ctxw[:].rearrange("p (j c) -> p j c", c=512),
                            s_ctxT.rearrange("(j p) t -> p j t", p=128)[
                                :, :, w * 512:(w + 1) * 512])
                        W("tensor", clc, clc.v)
                        for dt in range(32):
                            sl = dt % 2
                            W("sync", tD, wob_used[sl])
                            wlc = dma("sync", f"wobt{sl}",
                                wobt[sl][:].rearrange("p (a c) -> p a c", c=128),
                                s_wob.rearrange("(a p) c -> p a c", p=128)[
                                    :, :, dt * 128:(dt + 1) * 128])
                            W("tensor", wlc, wlc.v)
                            W("tensor", sD, o_evac[sl])
                            for j in range(32):
                                def fmm(t, sl=sl, j=j):
                                    inst = t.matmul(
                                        ps_o[sl][:], wobt[sl][:, j * 128:(j + 1) * 128],
                                        ctxw[:, j * 512:(j + 1) * 512],
                                        start=(j == 0), stop=(j == 31))
                                    if j == 31:
                                        tD.inc(inst)
                                E("tensor", fmm)
                            wob_used[sl] = tD.v
                            t_o = tD.v
                            W("scalar", tD, t_o)
                            E("scalar", lambda sc_, sl=sl, dt=dt: sD.inc(
                                sc_.copy(attnw[:, dt * 512:(dt + 1) * 512], ps_o[sl][:])))
                            o_evac[sl] = sD.v
                        ctx_used_t = tD.v
                        W("tensor", sD, max(o_evac))
                        for tt in range(4):
                            xs = tt % 2
                            W("sync", vD, x_used[xs])
                            xlc = dma("sync", f"xw{xs}", xw[xs][:],
                                xown[(w * 4 + tt) * 128:(w * 4 + tt + 1) * 128, :])
                            xl = xlc.v
                            for grp in range(8):
                                sl = grp % 2
                                W("tensor", vD, t_free[sl])
                                for j4 in range(4):
                                    dt = grp * 4 + j4
                                    def fmm(t, sl=sl, dt=dt, j4=j4, tt=tt):
                                        inst = t.matmul(
                                            ps_t[sl][:, j4 * 128:(j4 + 1) * 128],
                                            attnw[:, dt * 512 + tt * 128:
                                                  dt * 512 + (tt + 1) * 128],
                                            ident[:], start=True, stop=True)
                                        if j4 == 3:
                                            tD.inc(inst)
                                    E("tensor", fmm)
                                t_t = tD.v
                                W("vector", tD, t_t)
                                W("vector", xlc, xl)
                                if grp == 0 and acc_spill[tt % 2] is not None:
                                    W("vector", *acc_spill[tt % 2])
                                E("vector", lambda v, sl=sl, grp=grp, tt=tt: vD.inc(
                                    v.tensor_tensor(
                                        accst[tt % 2][:, grp * 512:(grp + 1) * 512],
                                        ps_t[sl][:], xw[tt % 2][:, grp * 512:(grp + 1) * 512],
                                        op=ALU.add)))
                                t_free[sl] = vD.v
                                if grp == 7:
                                    x_used[tt % 2] = vD.v
                            W("gpsimd", vD, x_used[tt % 2])
                            ac = dma("gpsimd", f"accsp{tt % 2}",
                                s_acc[(w * 4 + tt) * 128:(w * 4 + tt + 1) * 128, :],
                                accst[tt % 2][:])
                            acc_spill[tt % 2] = (ac, ac.v)
                    drain_dma("gpsimd")
                    drain_dma("sync")
                    barrier()
            _phase5()

            if PHASE_LIMIT <= 5:
                raise _StopBuild()

            def _phase6():
                with contextlib.ExitStack() as sc:
                    pk = [sc.enter_context(nc.sbuf_tensor(f"e_pk{i}", [128, 8 * 128], I32))
                          for i in range(2)]
                    hist = [sc.enter_context(nc.sbuf_tensor(f"e_hi{i}", [128, 128], I32))
                            for i in range(4)]
                    stg = [sc.enter_context(
                        nc.sbuf_tensor(f"e_st{i}", [128, 16 * 128], BF16))
                        for i in range(2)]
                    st4g = [sc.enter_context(nc.sbuf_tensor(f"e_s4{i}", [128, 8 * 128], F32))
                            for i in range(2)]

                    JG = [(0, 8), (8, 8), (16, 8), (24, 8), (32, 8), (40, 3)]
                    pk_used = [0, 0]
                    stg_spill = [None, None]

                    st4_used_v = [0, 0]
                    for hh in range(32):
                        for gi, (j0, glen) in enumerate(JG):
                            psl = gi % 2
                            W("sync", vD, st4_used_v[psl])
                            s4c = dma("sync", f"st4g{psl}",
                                st4g[psl][:, 0:glen * 128].rearrange(
                                    "p (a c) -> p a c", c=128),
                                d_sT4e.rearrange("p (a c) -> p a c", c=D)[
                                    :, j0:j0 + glen, hh * 128:(hh + 1) * 128])
                            W("vector", s4c, s4c.v)
                            W("sync", vD, pk_used[psl])
                            plc = dma("sync", f"pkE{psl}",
                                pk[psl][:, 0:glen * 128].rearrange(
                                    "p (a c) -> p a c", c=128),
                                d_wqT.rearrange("(a p) c -> p a c", p=128)[
                                    :, j0:j0 + glen, hh * 128:(hh + 1) * 128])
                            sgl = gi % 2
                            if stg_spill[sgl] is not None:
                                W("vector", *stg_spill[sgl])
                            W("vector", plc, plc.v)
                            def ext(jj, psl=psl):
                                hA = 2 * (jj % 2)
                                E("vector", lambda v, psl=psl, jj=jj, hA=hA: (
                                    v.tensor_scalar(hist[hA][:],
                                                    pk[psl][:, jj * 128:(jj + 1) * 128],
                                                    4, None, op0=ALU.logical_shift_right)))
                                E("vector", lambda v, psl=psl, jj=jj, hA=hA: (
                                    v.tensor_scalar(hist[hA + 1][:],
                                                    pk[psl][:, jj * 128:(jj + 1) * 128],
                                                    15, None, op0=ALU.bitwise_and)))

                            def mults(jj, last, psl=psl, sgl=sgl):
                                hA = 2 * (jj % 2)
                                E("vector", lambda v, hA=hA, psl=psl, sgl=sgl, jj=jj: (
                                    v.tensor_tensor(stg[sgl][:, (2 * jj) * 128:
                                                             (2 * jj + 1) * 128],
                                                    hist[hA][:],
                                                    st4g[psl][:, jj * 128:(jj + 1) * 128],
                                                    op=ALU.mult)))
                                def flo(v, hA=hA, psl=psl, sgl=sgl, jj=jj):
                                    inst = v.tensor_tensor(
                                        stg[sgl][:, (2 * jj + 1) * 128:
                                                 (2 * jj + 2) * 128],
                                        hist[hA + 1][:],
                                        st4g[psl][:, jj * 128:(jj + 1) * 128],
                                        op=ALU.mult)
                                    if last:
                                        vD.inc(inst)
                                E("vector", flo)

                            ext(0)
                            for jj in range(glen):
                                if jj + 1 < glen:
                                    ext(jj + 1)
                                mults(jj, last=(jj == glen - 1))
                            pk_used[psl] = vD.v
                            st4_used_v[psl] = vD.v
                            W("gpsimd", vD, vD.v)
                            wdc = dma("gpsimd", f"wdsp{sgl}",
                                s_wdT[256 * j0:256 * (j0 + glen),
                                      hh * 128:(hh + 1) * 128].rearrange(
                                    "(a p) c -> p a c", p=128),
                                stg[sgl][:, 0:2 * glen * 128].rearrange(
                                    "p (a c) -> p a c", c=128))
                            stg_spill[sgl] = (wdc, wdc.v)
                    drain_dma("gpsimd")
                    drain_dma("sync")
                    barrier()
            _phase6()

            if PHASE_LIMIT <= 6:
                raise _StopBuild()

            def _phase7():
                with contextlib.ExitStack() as sc:
                    xne = sc.enter_context(nc.sbuf_tensor("f_xe", [128, 16 * 2048], BF16))
                    xno = sc.enter_context(nc.sbuf_tensor("f_xo", [128, 16 * 2048], BF16))
                    zsb = [sc.enter_context(nc.sbuf_tensor(f"f_zs{i}", [64, 256], BF16))
                           for i in range(2)]
                    pkk = {"g": sc.enter_context(nc.sbuf_tensor("f_pkg", [128, 2048], I32)),
                           "u": sc.enter_context(nc.sbuf_tensor("f_pku", [128, 2048], I32))}
                    hist = [sc.enter_context(nc.sbuf_tensor(f"f_hi{i}", [128, 128], I32))
                            for i in range(4)]
                    planes = {"g": [sc.enter_context(
                        nc.sbuf_tensor(f"f_plg{i}", [128, 32 * 128], BF16))
                        for i in range(2)],
                        "u": [sc.enter_context(
                            nc.sbuf_tensor(f"f_plu{i}", [128, 32 * 128], BF16))
                            for i in range(2)]}
                    st4 = {"g": sc.enter_context(nc.sbuf_tensor("f_s4g", [4, 2048], F32)),
                           "u": sc.enter_context(nc.sbuf_tensor("f_s4u", [4, 2048], F32))}
                    hstg = sc.enter_context(nc.sbuf_tensor("f_h", [128, 2048], BF16))
                    silu_t = sc.enter_context(nc.sbuf_tensor("f_sl", [128, 512], BF16))
                    ps_g = [sc.enter_context(nc.psum_tensor(f"f_pg{i}", [128, 512], F32))
                            for i in range(2)]
                    ps_u = [sc.enter_context(nc.psum_tensor(f"f_pu{i}", [128, 512], F32))
                            for i in range(2)]
                    ps_se = [sc.enter_context(nc.psum_tensor(f"f_se{i}", [128, 512], F32))
                             for i in range(2)]

                    e1 = dma("sync", "xneL", xne[:].rearrange("p (j c) -> p j c", c=2048),
                        s_xnTe.rearrange("(j p) t -> p j t", p=128))
                    e2 = dma("sync", "xnoL", xno[:].rearrange("p (j c) -> p j c", c=2048),
                        s_xnTo.rearrange("(j p) t -> p j t", p=128))
                    W("tensor", e1, e1.v)
                    W("tensor", e2, e2.v)

                    pk_used = {"g": 0, "u": 0}
                    zs_used = [0, 0]
                    st4_used = {"g": 0, "u": 0}
                    pl_used = {"g": [0, 0], "u": [0, 0]}
                    se_used = [0, 0]
                    gu_free = {"g": [0, 0], "u": [0, 0]}
                    h_spill = None

                    for k in range(NKC):
                        pls = k % 2
                        zss = k % 2
                        W("sync", tD, zs_used[zss])
                        z1 = dma("sync", f"zsbG{zss}", zsb[zss][:, 0:128],
                            s_zs[:, k * 128:(k + 1) * 128])
                        z2 = dma("sync", f"zsbU{zss}", zsb[zss][:, 128:256],
                            s_zs[:, INTER + k * 128:INTER + (k + 1) * 128])
                        for wname, wt, s4t in (("g", g_wqT, g_sT4), ("u", u_wqT, u_sT4)):
                            W("sync", vD, pk_used[wname])
                            pkc = dma("sync", f"pkk{wname}",
                                pkk[wname][:].rearrange("p (a c) -> p a c", c=128),
                                wt.rearrange("(a p) c -> p a c", p=128)[
                                    :, :, k * 128:(k + 1) * 128])
                            W("sync", tD, st4_used[wname])
                            s4c = dma("sync", f"st4{wname}",
                                st4[wname][:].rearrange("p (a c) -> p a c", c=128),
                                s4t.rearrange("p (a c) -> p a c", c=INTER)[
                                    :, :, k * 128:(k + 1) * 128])
                            W("vector", pkc, pkc.v)
                            W("tensor", s4c, s4c.v)
                            def ext(jt, wname=wname):
                                hA = 2 * (jt % 2)
                                E("vector", lambda v, wname=wname, jt=jt, hA=hA: (
                                    v.tensor_scalar(hist[hA][:],
                                                    pkk[wname][:, jt * 128:(jt + 1) * 128],
                                                    4, None, op0=ALU.logical_shift_right)))
                                E("vector", lambda v, wname=wname, jt=jt, hA=hA: (
                                    v.tensor_scalar(hist[hA + 1][:],
                                                    pkk[wname][:, jt * 128:(jt + 1) * 128],
                                                    15, None, op0=ALU.bitwise_and)))

                            for jt in range(16):
                                ses = (jt // 4) % 2
                                co = (jt % 4) * 128
                                if jt % 4 == 0:
                                    W("tensor", vD, se_used[ses])
                                    def fse(t, ses=ses, jt=jt, wname=wname):
                                        tD.inc(t.matmul(
                                            ps_se[ses][:, 0:512], ind4[:],
                                            st4[wname][0:4, jt * 128:(jt + 4) * 128],
                                            start=True, stop=True))
                                    E("tensor", fse)
                                    if jt == 12:
                                        st4_used[wname] = tD.v
                                    W("vector", tD, tD.v)
                                if jt == 0:
                                    W("vector", tD, pl_used[wname][pls])
                                    ext(0)
                                if jt + 1 < 16:
                                    ext(jt + 1)
                                hA = 2 * (jt % 2)
                                E("vector", lambda v, wname=wname, jt=jt, hA=hA,
                                  ses=ses, pls=pls, co=co: (
                                    v.tensor_tensor(
                                        planes[wname][pls][:, (2 * jt) * 128:
                                                           (2 * jt + 1) * 128],
                                        hist[hA][:], ps_se[ses][:, co:co + 128],
                                        op=ALU.mult)))
                                E("vector", lambda v, wname=wname, jt=jt, hA=hA,
                                  ses=ses, pls=pls, co=co: vD.inc(
                                    v.tensor_tensor(
                                        planes[wname][pls][:, (2 * jt + 1) * 128:
                                                           (2 * jt + 2) * 128],
                                        hist[hA + 1][:], ps_se[ses][:, co:co + 128],
                                        op=ALU.mult)))
                                se_used[ses] = vD.v
                            pk_used[wname] = vD.v
                        pl_ready = vD.v

                        W("tensor", vD, pl_ready)
                        for q in range(4):
                            qs = q % 2
                            for wname, pspair in (("g", ps_g), ("u", ps_u)):
                                W("tensor", sD if wname == "g" else vD,
                                  gu_free[wname][qs])
                                for pt in range(32):
                                    jt, par = pt // 2, pt % 2
                                    xbuf_eo = xne if par == 0 else xno
                                    def fmm(t, wname=wname, pls=pls, pt=pt, jt=jt,
                                            q=q, qs=qs, xbuf_eo=xbuf_eo, pspair=pspair):
                                        t.matmul(
                                            pspair[qs][:],
                                            planes[wname][pls][:, pt * 128:(pt + 1) * 128],
                                            xbuf_eo[:, jt * 2048 + q * 512:
                                                    jt * 2048 + (q + 1) * 512],
                                            start=(pt == 0), stop=False)
                                    E("tensor", fmm)
                                if wname == "g" and q == 0:
                                    W("tensor", z1, z1.v)
                                    W("tensor", z2, z2.v)
                                def fzs(t, wname=wname, q=q, qs=qs, pspair=pspair,
                                        zss=zss):
                                    c0 = 0 if wname == "g" else 128
                                    tD.inc(t.matmul(
                                        pspair[qs][:],
                                        zsb[zss][:, c0:c0 + 128],
                                        sxn[:, q * 512:(q + 1) * 512],
                                        start=False, stop=True))
                                E("tensor", fzs)
                                if wname == "g":
                                    t_g = tD.v
                                else:
                                    t_u = tD.v
                            W("scalar", tD, t_g)
                            E("scalar", lambda sc_, qs=qs: sD.inc(
                                sc_.activation(silu_t[:], ps_g[qs][:], AF.Silu)))
                            s_sl = sD.v
                            gu_free["g"][qs] = s_sl
                            W("vector", sD, s_sl)
                            W("vector", tD, t_u)
                            if q == 0 and h_spill is not None:
                                W("vector", *h_spill)
                            E("vector", lambda v, qs=qs, q=q: vD.inc(
                                v.tensor_tensor(hstg[:, q * 512:(q + 1) * 512],
                                                silu_t[:], ps_u[qs][:], op=ALU.mult)))
                            gu_free["u"][qs] = vD.v
                        pl_used["g"][pls] = tD.v
                        pl_used["u"][pls] = tD.v
                        zs_used[zss] = tD.v
                        W("gpsimd", vD, vD.v)
                        hc = dma("gpsimd", "hsp", s_hT[k * 128:(k + 1) * 128, :],
                                 hstg[:])
                        h_spill = (hc, hc.v)
                    drain_dma("gpsimd")
                    drain_dma("sync")
                    barrier()
            _phase7()

            if PHASE_LIMIT <= 7:
                raise _StopBuild()

            def _phase8():
                with contextlib.ExitStack() as sc:
                    hT = sc.enter_context(nc.sbuf_tensor("g_h", [128, NKC * 512], BF16))
                    wdt = [sc.enter_context(
                        nc.sbuf_tensor(f"g_wd{i}", [128, 43 * 128], BF16))
                        for i in range(2)]
                    shA = sc.enter_context(nc.sbuf_tensor("g_shA", [128, 512], BF16))
                    shB = sc.enter_context(nc.sbuf_tensor("g_shB", [44, 512], BF16))
                    ffnw = sc.enter_context(nc.sbuf_tensor("g_ff", [128, 32 * 512], BF16))
                    acct = [sc.enter_context(nc.sbuf_tensor(f"g_ac{i}", [128, D], F32))
                            for i in range(1)]
                    bo_b = sc.enter_context(nc.sbuf_tensor("g_bo", [128, D], F32))
                    bo_s = sc.enter_context(nc.sbuf_tensor("g_bos", [1, D], F32))
                    zsda = [sc.enter_context(nc.sbuf_tensor(f"g_za{i}", [128, 128], BF16))
                            for i in range(2)]
                    zsdb = [sc.enter_context(nc.sbuf_tensor(f"g_zb{i}", [44, 128], BF16))
                            for i in range(2)]
                    ps_dn = [sc.enter_context(nc.psum_tensor(f"g_pd{i}", [128, 512], F32))
                             for i in range(2)]
                    ps_f = [sc.enter_context(nc.psum_tensor(f"g_pf{i}", [128, 512], F32))
                            for i in range(2)]
                    ps_sh = sc.enter_context(nc.psum_tensor("g_psh", [128, 512], F32))
                    ps_sh2 = sc.enter_context(nc.psum_tensor("g_psh2", [128, 512], F32))
                    ps_bo = sc.enter_context(nc.psum_tensor("g_pbo", [128, 512], F32))

                    boc = dma("sync", "boL", bo_s[:], bo)
                    W("tensor", boc, boc.v)
                    bo_marks = []
                    for c8 in range(8):
                        def fbo(t, c8=c8):
                            tD.inc(t.matmul(ps_bo[:], ones1[:],
                                            bo_s[0:1, c8 * 512:(c8 + 1) * 512],
                                            start=True, stop=True))
                        E("tensor", fbo)
                        t_bo = tD.v
                        W("vector", tD, t_bo)
                        E("vector", lambda v, c8=c8: vD.inc(
                            v.tensor_copy(bo_b[:, c8 * 512:(c8 + 1) * 512], ps_bo[:])))
                        W("tensor", vD, vD.v)

                    dn_evac = [0, 0]
                    zsd_used = [0, 0]
                    f_used = [0, 0]
                    wdt_used = [0, 0]
                    acc_out = [None, None]
                    sh_done = 0
                    hT_used = 0

                    for w in range(4):
                        W("sync", tD, hT_used)
                        hlc = dma("sync", "hTL",
                            hT[:].rearrange("p (kk c) -> p kk c", c=512),
                            s_hT.rearrange("(kk p) t -> p kk t", p=128)[
                                :, :, w * 512:(w + 1) * 512])
                        W("tensor", hlc, hlc.v)
                        if sh_done:
                            W("tensor", vD, sh_done)
                        for part, lim, pst in (("A", 32, ps_sh), ("B", 11, ps_sh2)):
                            nmm = 2 * lim
                            cnt = 0
                            for m in range(lim):
                                mm0 = m if part == "A" else m + 32
                                for e in range(2):
                                    k = 2 * mm0 + e
                                    cnt += 1
                                    def fsh(t, m=m, k=k, pst=pst,
                                            st=(cnt == 1), sp=(cnt == nmm)):
                                        inst = t.matmul(
                                            pst[:, :],
                                            f132[:, 128 - 4 * m:256 - 4 * m],
                                            hT[:, k * 512:(k + 1) * 512],
                                            start=st, stop=sp)
                                        if sp:
                                            tD.inc(inst)
                                    E("tensor", fsh)
                            t_sh = tD.v
                            W("vector", tD, t_sh)
                            if part == "A":
                                E("vector", lambda v: vD.inc(
                                    v.tensor_copy(shA[:], ps_sh[:])))
                            else:
                                E("vector", lambda v: vD.inc(
                                    v.tensor_copy(shB[:], ps_sh2[0:44, :])))
                            sh_done = vD.v

                        W("tensor", vD, sh_done)
                        for hh in range(32):
                            zl = hh % 2
                            if w > 0 or hh > 1:
                                W("sync", tD, zsd_used[zl])
                            zd1 = dma("sync", f"zsdA{zl}", zsda[zl][:],
                                      s_zsd[0:128, hh * 128:(hh + 1) * 128])
                            zd2 = dma("sync", f"zsdB{zl}", zsdb[zl][:],
                                      s_zsd[128:172, hh * 128:(hh + 1) * 128])
                            for half in range(2):
                                sl = half
                                W("sync", tD, wdt_used[sl])
                                wlc = dma("sync", f"wdt{sl}",
                                    wdt[sl][:].rearrange("p (a c) -> p a c", c=128),
                                    s_wdT.rearrange("(a p) c -> p a c", p=128)[
                                        :, half * 43:half * 43 + 43,
                                        hh * 128:(hh + 1) * 128])
                                W("tensor", wlc, wlc.v)
                                if half == 0:
                                    W("tensor", sD, dn_evac[hh % 2])
                                for kk in range(43):
                                    k = half * 43 + kk
                                    def fmm(t, sl=sl, kk=kk, k=k, hh=hh, half=half):
                                        inst = t.matmul(
                                            ps_dn[hh % 2][:],
                                            wdt[sl][:, kk * 128:(kk + 1) * 128],
                                            hT[:, k * 512:(k + 1) * 512],
                                            start=(half == 0 and kk == 0), stop=False)
                                        if kk == 42:
                                            tD.inc(inst)
                                    E("tensor", fmm)
                                wdt_used[sl] = tD.v
                            W("tensor", zd1, zd1.v)
                            W("tensor", zd2, zd2.v)
                            def fzs1(t, zl=zl, hh=hh):
                                t.matmul(ps_dn[hh % 2][:], zsda[zl][:], shA[:],
                                         start=False, stop=False)
                            E("tensor", fzs1)
                            def fzs2(t, zl=zl, hh=hh):
                                tD.inc(t.matmul(ps_dn[hh % 2][:],
                                                zsdb[zl][0:44, :],
                                                shB[0:44, :],
                                                start=False, stop=True))
                            E("tensor", fzs2)
                            t_dn = tD.v
                            zsd_used[zl] = tD.v
                            W("scalar", tD, t_dn)
                            E("scalar", lambda sc_, hh=hh: sD.inc(
                                sc_.copy(ffnw[:, hh * 512:(hh + 1) * 512],
                                         ps_dn[hh % 2][:])))
                            dn_evac[hh % 2] = sD.v
                        hT_used = tD.v
                        W("tensor", sD, max(dn_evac))
                        for tt in range(4):
                            asl = 0
                            if acc_out[asl] is not None:
                                W("sync", *acc_out[asl])
                            alc = dma("sync", "accL", acct[asl][:],
                                s_acc[(w * 4 + tt) * 128:(w * 4 + tt + 1) * 128, :])
                            W("vector", alc, alc.v)
                            E("vector", lambda v, asl=asl: vD.inc(
                                v.tensor_tensor(acct[asl][:], acct[asl][:], bo_b[:],
                                                op=ALU.add)))
                            for grp in range(8):
                                fsl = grp % 2
                                W("tensor", vD, f_used[fsl])
                                for j4 in range(4):
                                    hh = grp * 4 + j4
                                    def fmm(t, fsl=fsl, hh=hh, j4=j4, tt=tt):
                                        inst = t.matmul(
                                            ps_f[fsl][:, j4 * 128:(j4 + 1) * 128],
                                            ffnw[:, hh * 512 + tt * 128:
                                                 hh * 512 + (tt + 1) * 128],
                                            ident[:], start=True, stop=True)
                                        if j4 == 3:
                                            tD.inc(inst)
                                    E("tensor", fmm)
                                t_f = tD.v
                                W("vector", tD, t_f)
                                E("vector", lambda v, asl=asl, grp=grp, fsl=fsl: vD.inc(
                                    v.tensor_tensor(
                                        acct[asl][:, grp * 512:(grp + 1) * 512],
                                        acct[asl][:, grp * 512:(grp + 1) * 512],
                                        ps_f[fsl][:], op=ALU.add)))
                                f_used[fsl] = vD.v
                            E("vector", lambda v: vD.inc(v.drain()))
                            W("sync", vD, vD.v)
                            yc = dma("sync", "yout",
                                y[(w * 4 + tt) * 128:(w * 4 + tt + 1) * 128, :],
                                acct[asl][:])
                            acc_out[asl] = (yc, yc.v)

                    drain_dma("sync")
                    drain_dma("gpsimd")
                    barrier()
            _phase8()

            if PHASE_LIMIT <= 8:
                raise _StopBuild()


        try:
            _run_phases()
        except _StopBuild:
            pass

        for name in DBG:
            ap, shape, dt_ = dbg_tensors[name]
            o = nc.dram_tensor("dbg_" + name, shape, F32,
                               kind="ExternalOutput").ap()
            dma("gpsimd", "dbg" + name, o, ap)
        drain_dma("sync")
        drain_dma("gpsimd")
        if True:
            pass

        with nc.Block() as block:
            for eng_name, lst in prog.items():
                def runner(e, lst=lst):
                    for fn in lst:
                        fn(e)
                getattr(block, eng_name)(runner)

    return nc


_NC_CACHE = None


def _host_prep(inputs, core):
    x = np.asarray(inputs["L0_attn_x"])
    Wq = np.asarray(inputs["L0_attn_Wq"])
    Wkv = np.asarray(inputs["L0_attn_Wkv"])
    Wo = np.asarray(inputs["L0_attn_Wo"])
    bo = np.asarray(inputs["L0_attn_bo"])
    sig = _sigma()
    b1 = core // 2
    par = core % 2

    xall = np.ascontiguousarray(x.reshape(NT, D))
    xown = np.ascontiguousarray(
        x[b1].reshape(8, GW, D)[par::2].reshape(2048, D))
    g0 = 2 * b1
    wkv_k = np.ascontiguousarray(np.concatenate(
        [Wkv[:, g0 * 256:g0 * 256 + 128],
         Wkv[:, (g0 + 1) * 256:(g0 + 1) * 256 + 128]], axis=1))
    wkv_v = np.ascontiguousarray(np.concatenate(
        [Wkv[:, g0 * 256 + 128:g0 * 256 + 256],
         Wkv[:, (g0 + 1) * 256 + 128:(g0 + 1) * 256 + 256]], axis=1))

    gq = np.ascontiguousarray(np.asarray(inputs["L0_ffn_gate_Wq"]).T[:, sig])
    uq = np.ascontiguousarray(np.asarray(inputs["L0_ffn_up_Wq"]).T[:, sig])
    dq = np.ascontiguousarray(np.asarray(inputs["L0_ffn_down_Wq"]).T)

    def sT4_of(scale, perm, njt):
        sT = scale[perm].T if perm is not None else scale.T
        return np.ascontiguousarray(
            sT.reshape(njt, 4, sT.shape[1]).transpose(1, 0, 2).reshape(
                4, njt * sT.shape[1]))

    g_scale = np.asarray(inputs["L0_ffn_gate_scale"])
    g_zero = np.asarray(inputs["L0_ffn_gate_zero"])
    u_scale = np.asarray(inputs["L0_ffn_up_scale"])
    u_zero = np.asarray(inputs["L0_ffn_up_zero"])
    d_scale = np.asarray(inputs["L0_ffn_down_scale"])
    d_zero = np.asarray(inputs["L0_ffn_down_zero"])

    g_sT4 = sT4_of(g_scale, sig, 16)
    u_sT4 = sT4_of(u_scale, sig, 16)
    d_sT4e = np.ascontiguousarray(
        np.repeat(sT4_of(d_scale, None, 43), 32, axis=0))
    g_sTf = np.ascontiguousarray(g_scale[sig].T)
    g_zTf = np.ascontiguousarray(g_zero[sig].T)
    u_sTf = np.ascontiguousarray(u_scale[sig].T)
    u_zTf = np.ascontiguousarray(u_zero[sig].T)
    d_sT = d_scale.T
    d_zT = d_zero.T
    d_sTa = np.ascontiguousarray(d_sT[0:128])
    d_sTb = np.ascontiguousarray(d_sT[128:172])
    d_zTa = np.ascontiguousarray(d_zT[0:128])
    d_zTb = np.ascontiguousarray(d_zT[128:172])

    bf = ml_dtypes.bfloat16
    identv = np.eye(128, dtype=bf)
    bm = np.zeros((4, 128, 32, 512), dtype=bf)
    for w in range(4):
        Q0 = (2 * w + par) * GW
        for kk in range(NV[w]):
            k0 = kk * 128
            if k0 + 128 <= Q0:
                bm[w, :, kk, :] = 1
            elif k0 >= Q0 + GW:
                pass
            else:
                m = (k0 - Q0) // 128
                p = np.arange(128)[:, None]
                f = np.arange(512)[None, :]
                bm[w, :, kk, :] = (128 * m + p <= f).astype(bf)
    bigmask = np.ascontiguousarray(bm.reshape(4 * 128, 32 * 512))

    ind4 = np.zeros((4, 128), np.float32)
    for p in range(128):
        ind4[p // 32, p] = 1.0
    f132 = np.zeros((128, 256), dtype=bf)
    for p in range(128):
        f132[p, 128 + p // 32] = 1
    g66 = np.zeros((128, 130), dtype=bf)
    for p in range(128):
        g66[p, 64 + p // 64] = 1
    onesv = np.ones((128, 128), dtype=bf)
    ones1f = np.ones((1, 128), np.float32)

    return {
        "xall": xall, "xown": xown, "wq": Wq, "wkv_k": wkv_k, "wkv_v": wkv_v,
        "wo": Wo, "bo": np.ascontiguousarray(bo.reshape(1, D)),
        "g_wqT": gq, "u_wqT": uq, "d_wqT": dq,
        "g_sT4": g_sT4, "u_sT4": u_sT4, "d_sT4e": d_sT4e,
        "g_sTf": g_sTf, "g_zTf": g_zTf, "u_sTf": u_sTf, "u_zTf": u_zTf,
        "d_sTa": d_sTa, "d_sTb": d_sTb, "d_zTa": d_zTa, "d_zTb": d_zTb,
        "identd": identv, "bigmask": bigmask, "ind4d": ind4,
        "f132d": f132, "g66d": g66, "onesd": onesv, "ones1f": ones1f,
    }


def kernel(**inputs):
    global _NC_CACHE
    if _NC_CACHE is None:
        _NC_CACHE = build()
    nc = _NC_CACHE
    in_maps = [_host_prep(inputs, core) for core in range(8)]
    res = run_bass_kernel_spmd(nc, in_maps, core_ids=list(range(8)))
    out = np.empty((BATCH, T, D), np.float32)
    for core in range(8):
        b1, par = core // 2, core % 2
        yv = np.asarray(res.results[core]["y"], np.float32)
        for w in range(4):
            q0 = (2 * w + par) * GW
            out[b1, q0:q0 + GW] = yv[w * GW:(w + 1) * GW]
    return out
